# version 01
# baseline (speedup 1.0000x reference)
import math
from contextlib import ExitStack

import numpy as np
import ml_dtypes

import concourse.bass as bass
import concourse.mybir as mybir
from concourse.bass_utils import run_bass_kernel_spmd
from concourse.aot_env import allow_ici_all_to_all

F32 = mybir.dt.float32
BF16 = mybir.dt.bfloat16
AF = mybir.ActivationFunctionType
OP = mybir.AluOpType
AX = mybir.AxisListType
BF = ml_dtypes.bfloat16

NCORE = 8
D = 2048
S = 8192
TC = S // NCORE
HD = 64
NE = 32
LAT = 1024
SHI = 4096
G = 4
TOP_K = 8
RSCALE = 2.5
LI = 0.8 - 0.6 * math.exp(-0.3 * 12)
EPS_LN = 1e-5
EPS_RMS = 1e-8
P3N = 8
P5STAGE = 3
P2SER = False
QK_SCALE = 64.0 ** -0.25
KSC = 2.0 ** 30


def _host_consts():
    d = np.arange(128) % 64
    j = d // 2
    inv_freq = 1.0 / (10000.0 ** (2.0 * j / 64.0))
    t = np.arange(S, dtype=np.float64)
    ang = inv_freq[:, None] * t[None, :]
    cos_t = (np.cos(ang) * QK_SCALE).astype(BF)
    sin_t = (np.sin(ang) * QK_SCALE).astype(BF)
    JT = np.zeros((128, 128), dtype=np.float32)
    for pp in range(0, 128, 2):
        JT[pp + 1, pp] = -1.0
        JT[pp, pp + 1] = 1.0
    masks = np.zeros((4, 128, 1024), dtype=np.float32)
    sp = np.arange(128)[:, None]
    tq = np.arange(512)[None, :]
    for a in range(4):
        m = (tq >= sp + 128 * a).astype(np.float32)
        masks[a, :, 0:512] = m
        masks[a, :, 512:1024] = m
    iden = np.eye(128, dtype=np.float32)
    return {
        "rope_cos": cos_t, "rope_sin": sin_t,
        "jt": JT.astype(BF),
        "masks": masks.astype(BF).reshape(512, 1024),
        "iden_f32": iden,
    }


def _prep(inputs):
    g = lambda k: np.asarray(inputs[k])
    x = np.ascontiguousarray(g("L0_attn_x").reshape(S, D))
    shared = {
        "wo": np.ascontiguousarray(g("L0_attn_Wo")),
        "lam_q1": g("L0_attn_lambda_q1").reshape(1, HD).copy(),
        "lam_k1": g("L0_attn_lambda_k1").reshape(1, HD).copy(),
        "lam_q2": g("L0_attn_lambda_q2").reshape(1, HD).copy(),
        "lam_k2": g("L0_attn_lambda_k2").reshape(1, HD).copy(),
        "ln_w": g("L0_attn_ln_w").reshape(128, 1).copy(),
        "ln_b": g("L0_attn_ln_b").reshape(128, 1).copy(),
        "gate_wT": np.ascontiguousarray(g("L0_ffn_gate_w").T),
        "gate_bias": g("L0_ffn_gate_bias").reshape(1, NE).copy(),
        "fc1T": np.ascontiguousarray(g("L0_ffn_fc1_w").T),
        "fc2T": np.ascontiguousarray(g("L0_ffn_fc2_w").T),
        "w1T": np.ascontiguousarray(g("L0_ffn_w1").transpose(0, 2, 1)),
        "w2T": np.ascontiguousarray(g("L0_ffn_w2").transpose(0, 2, 1)),
        "upT": np.ascontiguousarray(g("L0_ffn_shared_up_w").T),
        "downT": np.ascontiguousarray(g("L0_ffn_shared_down_w").T),
    }
    shared.update(_host_consts())
    wq, wk, wv = g("L0_attn_Wq"), g("L0_attn_Wk"), g("L0_attn_Wv")
    in_maps = []
    for c in range(NCORE):
        m = dict(shared)
        m["x_slice"] = np.ascontiguousarray(x[c * TC:(c + 1) * TC, :])
        sl = slice(256 * c, 256 * c + 256)
        m["wq_s"] = np.ascontiguousarray(wq[:, sl])
        m["wk_s"] = np.ascontiguousarray(wk[:, sl])
        m["wv_s"] = np.ascontiguousarray(wv[:, sl])
        in_maps.append(m)
    return in_maps


class P:
    ENGINES = ("tensor", "scalar", "vector", "gpsimd", "sync")

    def __init__(self, nc):
        self.nc = nc
        self.ops = {e: [] for e in self.ENGINES}
        self.cnt = {}

    def sem(self, name):
        self.cnt[name] = 0

    def on(self, eng, fn):
        self.ops[eng].append(fn)

    def wait(self, eng, sname, val):
        if val <= 0:
            return
        self.on(eng, lambda e, SM, s=sname, v=val: e.wait_ge(SM[s], v))

    def do(self, eng, sname, fn):
        self.cnt[sname] += 1
        self.on(eng, lambda e, SM, g=fn, s=sname: g(e).then_inc(SM[s], 1))
        return self.cnt[sname]

    def run(self, eng, fn):
        self.on(eng, lambda e, SM, g=fn: g(e))

    def dma(self, eng, sname, fn):
        self.cnt[sname] += 16
        self.on(eng, lambda e, SM, g=fn, s=sname: g(e).then_inc(SM[s], 16))
        return self.cnt[sname]

    def val(self, sname):
        return self.cnt[sname]


def build_program(debug=(), upto=99):
    nc = bass.Bass("TRN2", target_bir_lowering=False)
    dbg = set(debug)

    ap = {}
    def din(name, shape, dt):
        ap[name] = nc.dram_tensor(name, shape, dt, kind="ExternalInput").ap()
    def dout(name, shape, dt):
        ap[name] = nc.dram_tensor(name, shape, dt, kind="ExternalOutput").ap()
    def dint(name, shape, dt):
        ap[name] = nc.dram_tensor(name, shape, dt).ap()

    din("x_slice", [TC, D], F32)
    din("wq_s", [D, 256], F32); din("wk_s", [D, 256], F32); din("wv_s", [D, 256], F32)
    din("wo", [D, D], F32)
    din("lam_q1", [1, HD], F32); din("lam_k1", [1, HD], F32)
    din("lam_q2", [1, HD], F32); din("lam_k2", [1, HD], F32)
    din("ln_w", [128, 1], F32); din("ln_b", [128, 1], F32)
    din("gate_wT", [D, NE], F32); din("gate_bias", [1, NE], F32)
    din("fc1T", [D, LAT], F32); din("fc2T", [LAT, D], F32)
    din("w1T", [NE, LAT, LAT], F32); din("w2T", [NE, LAT, LAT], F32)
    din("upT", [D, SHI], F32); din("downT", [SHI, D], F32)
    din("rope_cos", [128, S], BF16); din("rope_sin", [128, S], BF16)
    din("jt", [128, 128], BF16)
    din("masks", [512, 1024], BF16)
    din("iden_f32", [128, 128], F32)
    dout("out", [TC, D], F32)

    dint("xln_contrib", [D, TC], BF16)
    dint("xln_gath", [NCORE * D, TC], BF16)
    dint("attn_contrib", [D, TC], BF16)
    dint("attn_slice", [D, TC], BF16)
    dint("x2_park", [TC, D], F32)
    dint("xln3_park", [D, TC], BF16)
    dint("cw_park", [NE, TC], F32)

    dbg_bufs = {}
    def ddbg(name, shape, dt=F32):
        if name in dbg:
            dbg_bufs[name] = nc.dram_tensor("dbg_" + name, shape, dt, kind="ExternalOutput").ap()
    ddbg("xln_fm", [D, TC], BF16)
    ddbg("x2", [TC, D]); ddbg("xln3", [D, TC], BF16)
    ddbg("cw", [NE, TC]); ddbg("xlat", [128, 8 * 1024], BF16)
    ddbg("ylat", [128, 8 * 1024]); ddbg("sh1", [128, 32 * 1024], BF16)
    ddbg("ffn", [128, 8 * 2048], BF16)
    ddbg("q_fm", [256, S], BF16); ddbg("k_fm", [256, S], BF16)
    ddbg("v_tm", [128, 64 * 256], BF16)
    ddbg("mhat", [128, 2])
    ddbg("lrows", [64, 512])
    ddbg("contrib", [D, TC], BF16)

    p = P(nc)
    mask_sb = [None]
    ctx = ExitStack()
    SB = lambda name, shape, dt: ctx.enter_context(nc.sbuf_tensor(name, shape, dt))
    sems = {}
    for s_ in ("pe", "act", "dve", "pool", "sd", "gd", "cc"):
        p.sem(s_)
        sems[s_] = ctx.enter_context(nc.semaphore("sem_" + s_))

    ones_col_bf = SB("ones_col_bf", [128, 1], BF16)
    ones_col_f = SB("ones_col_f", [128, 1], F32)
    ones_row_f = SB("ones_row_f", [1, 128], F32)
    iden_bf = SB("iden_bf", [128, 128], BF16)
    iden_f = SB("iden_f", [128, 128], F32)
    jt_sb = SB("jt_sb", [128, 128], BF16)
    lnw_eff = SB("lnw_eff", [128, 1], F32)
    lnb_eff = SB("lnb_eff", [128, 1], F32)
    lam_sb = SB("lam_sb", [1, 1], F32)
    mhat_neg = SB("mhat_neg", [128, 2], F32)
    gb_sb = SB("gb_sb", [128, NE], F32)
    tiny = SB("tiny", [1, 64], F32)
    tiny2 = SB("tiny2", [1, 64], F32)
    eps_ln_sb = SB("eps_ln_sb", [128, 1], F32)
    eps_rms_sb = SB("eps_rms_sb", [128, 1], F32)

    st = {}

    def phase0():
        ldA = SB("p0_ldA", [128, 1], F32)
        ldB = SB("p0_ldB", [128, 1], F32)
        lamv = [SB("p0_lam%d" % i, [1, 64], F32) for i in range(4)]
        gbrow = SB("p0_gbrow", [1, NE], F32)
        p.dma("sync", "sd", lambda e: e.dma_start(out=ldA[:], in_=ap["ln_w"]))
        p.dma("sync", "sd", lambda e: e.dma_start(out=ldB[:], in_=ap["ln_b"]))
        for i, nm in enumerate(("lam_q1", "lam_k1", "lam_q2", "lam_k2")):
            p.dma("sync", "sd", lambda e, i2=i, n=nm: e.dma_start(out=lamv[i2][:], in_=ap[n]))
        p.dma("sync", "sd", lambda e: e.dma_start(out=jt_sb[:], in_=ap["jt"]))
        p.dma("sync", "sd", lambda e: e.dma_start(out=iden_f[:], in_=ap["iden_f32"]))
        v_ld = p.dma("sync", "sd", lambda e: e.dma_start(out=gbrow[:], in_=ap["gate_bias"]))

        p.run("gpsimd", lambda e: e.memset(ones_col_bf[:], 1.0))
        p.run("gpsimd", lambda e: e.memset(eps_ln_sb[:], EPS_LN))
        p.run("gpsimd", lambda e: e.memset(eps_rms_sb[:], EPS_RMS))
        p.run("gpsimd", lambda e: e.memset(ones_col_f[:], 1.0))
        vg = p.do("gpsimd", "pool", lambda e: e.memset(ones_row_f[:], 1.0))
        st["p0_memset"] = vg

        p.wait("vector", "sd", v_ld)
        p.do("vector", "dve", lambda e: e.tensor_copy(iden_bf[:], iden_f[:]))
        p.do("vector", "dve", lambda e: e.tensor_scalar_mul(lnw_eff[:], ldA[:], 1.0 - LI))
        p.do("vector", "dve", lambda e: e.tensor_scalar_mul(lnb_eff[:], ldB[:], 1.0 - LI))
        p.do("vector", "dve", lambda e: e.tensor_mul(tiny[:], lamv[0][:], lamv[1][:]))
        vm = p.do("vector", "dve", lambda e: e.tensor_mul(tiny2[:], lamv[2][:], lamv[3][:]))
        p.wait("vector", "dve", vm)
        p.do("vector", "dve", lambda e: e.tensor_reduce(tiny[0:1, 0:1], tiny[:], AX.X, OP.add))
        v2 = p.do("vector", "dve", lambda e: e.tensor_reduce(tiny2[0:1, 0:1], tiny2[:], AX.X, OP.add))
        p.wait("scalar", "dve", v2)
        p.do("scalar", "act", lambda e: e.activation(tiny[0:1, 1:2], tiny[0:1, 0:1], AF.Exp))
        e2 = p.do("scalar", "act", lambda e: e.activation(tiny2[0:1, 1:2], tiny2[0:1, 0:1], AF.Exp))
        p.wait("vector", "act", e2)
        vs = p.do("vector", "dve", lambda e: e.tensor_sub(tiny[0:1, 2:3], tiny[0:1, 1:2], tiny2[0:1, 1:2]))
        p.wait("vector", "dve", vs)
        p.do("vector", "dve", lambda e: e.tensor_scalar_add(lam_sb[:], tiny[0:1, 2:3], LI))
        p0ps = ExitStack()
        gbp = p0ps.enter_context(nc.psum_tensor("p0_gbp", [128, NE], F32))
        p0ps.close()
        p.wait("tensor", "pool", vg)
        p.wait("tensor", "sd", v_ld)
        vmm = p.do("tensor", "pe", lambda e: e.matmul(gbp[:], ones_row_f[:], gbrow[:], start=True, stop=True))
        p.wait("vector", "pe", vmm)
        p.do("vector", "dve", lambda e: e.tensor_copy(gb_sb[:], gbp[:]))
        st["p0_dve"] = p.val("dve")

    def phase1(grp):
        grp1c = ExitStack()
        sb = lambda n, s_, d_: grp1c.enter_context(nc.sbuf_tensor(n, s_, d_))
        ps = lambda n, s_, d_: grp1c.enter_context(nc.psum_tensor(n, s_, d_))
        q_sb, k_sb, v_sb = st["q_sb"], st["k_sb"], st["v_sb"]

        grp1a = ExitStack()
        sba = lambda n, s_, d_: grp1a.enter_context(nc.sbuf_tensor(n, s_, d_))
        xt = [sba("p1_xt%d" % i, [128, D], F32) for i in range(2)]
        sqs = sba("p1_sqs", [128, D], F32)
        xln_tm = sba("p1_xlntm", [128, D], BF16)
        xlnfm = sba("p1_xlnfm", [128, 16 * TC], BF16)
        stats = sba("p1_stats", [128, 8], F32)
        tp_ps = [ps("p1_tp%d" % i, [128, 128], BF16) for i in range(2)]
        p.wait("tensor", "dve", st["p0_dve"])

        p.dma("sync", "sd", lambda e: e.dma_start(
            out=mask_sb[0][:], in_=ap["masks"].rearrange("(a p) t -> p a t", p=128)))
        evac_hist = []
        v_norm_prev = 0
        va_hist = [0, 0]
        for tt in range(8):
            buf = xt[tt % 2]
            p.wait("sync", "act", va_hist[tt % 2])
            vld = p.dma("sync", "sd", lambda e, b=buf, t=tt: e.dma_start(
                out=b[:], in_=ap["x_slice"][t * 128:(t + 1) * 128, :]))
            p.wait("vector", "sd", vld)
            vs1 = p.do("vector", "dve", lambda e, b=buf: e.tensor_reduce(
                stats[:, 0:1], b[:], AX.X, OP.add))
            p.wait("scalar", "sd", vld)
            vs2 = p.do("scalar", "act", lambda e, b=buf: e.activation(
                sqs[:], b[:], AF.Square))
            p.wait("vector", "act", vs2)
            vs2b = p.do("vector", "dve", lambda e: e.tensor_reduce(
                stats[:, 1:2], sqs[:], AX.X, OP.add))
            p.wait("vector", "dve", vs2b)
            p.wait("vector", "dve", vs1)
            vmu = p.do("vector", "dve", lambda e: e.tensor_scalar_mul(stats[:, 2:3], stats[:, 0:1], 1.0 / D))
            p.wait("vector", "dve", vmu)
            vw = p.do("vector", "dve", lambda e: e.tensor_mul(stats[:, 3:4], stats[:, 2:3], stats[:, 2:3]))
            p.wait("vector", "dve", vw)
            vv = p.do("vector", "dve", lambda e: e.scalar_tensor_tensor(
                stats[:, 4:5], stats[:, 1:2], 1.0 / D, stats[:, 3:4], OP.mult, OP.subtract))
            p.wait("scalar", "dve", vv)
            if tt == 0:
                p.wait("scalar", "pool", st["p0_memset"])
            vl = p.do("scalar", "act", lambda e: e.activation(stats[:, 5:6], stats[:, 4:5], AF.Ln, bias=eps_ln_sb[:]))
            p.wait("scalar", "act", vl)
            vr = p.do("scalar", "act", lambda e: e.activation(stats[:, 6:7], stats[:, 5:6], AF.Exp, scale=-0.5))
            p.wait("vector", "act", vr)
            vn = p.do("vector", "dve", lambda e: e.scalar_tensor_tensor(
                stats[:, 7:8], stats[:, 2:3], -1.0, stats[:, 6:7], OP.mult, OP.mult))
            p.wait("scalar", "dve", vn)
            p.wait("scalar", "pe", v_norm_prev)
            va = p.do("scalar", "act", lambda e, b=buf: e.activation(
                xln_tm[:], b[:], AF.Identity, bias=stats[:, 7:8], scale=stats[:, 6:7]))
            va_hist[tt % 2] = va
            p.wait("tensor", "act", va)
            for kc in range(16):
                slot = len(evac_hist) % 2
                pslot = tp_ps[slot]
                if len(evac_hist) >= 2:
                    p.wait("tensor", "dve", evac_hist[len(evac_hist) - 2])
                vt = p.do("tensor", "pe", lambda e, s2=pslot, k=kc: e.transpose(
                    s2[:], xln_tm[:, k * 128:(k + 1) * 128], iden_bf[:]))
                p.wait("vector", "pe", vt)
                ve = p.do("vector", "dve", lambda e, s2=pslot, k=kc, t=tt: e.tensor_copy(
                    xlnfm[:, k * TC + t * 128:k * TC + t * 128 + 128], s2[:]))
                evac_hist.append(ve)
            v_norm_prev = p.val("pe")

        p.wait("sync", "dve", evac_hist[-1])
        vc = p.dma("sync", "sd", lambda e: e.dma_start(
            out=ap["xln_contrib"].rearrange("(k p) t -> p k t", p=128), in_=xlnfm[:]))
        if "xln_fm" in dbg:
            p.dma("sync", "sd", lambda e: e.dma_start(
                out=dbg_bufs["xln_fm"].rearrange("(k p) t -> p k t", p=128), in_=xlnfm[:]))
        p.wait("gpsimd", "sd", p.val("sd"))
        vcc = p.do("gpsimd", "cc", lambda e: e.collective_compute(
            "AllGather", OP.bypass, replica_groups=[list(range(NCORE))],
            ins=[ap["xln_contrib"].opt()], outs=[ap["xln_gath"].opt()]))
        grp1a.close()

        wq_sb = sb("p1_wq", [128, 16 * 256], BF16)
        wk_sb = sb("p1_wk", [128, 16 * 256], BF16)
        wv_sb = sb("p1_wv", [128, 16 * 256], BF16)
        for nm, t_ in (("wq_s", wq_sb), ("wk_s", wk_sb), ("wv_s", wv_sb)):
            p.dma("gpsimd", "gd", lambda e, n=nm, t2=t_: e.dma_start(
                out=t2[:], in_=ap[n].rearrange("(k p) m -> p k m", p=128)))
        vwts = p.val("gd")

        xbuf = [sb("p1_xbuf%d" % i, [128, 16 * 512], BF16) for i in range(2)]
        ctab = [sb("p1_ctab%d" % i, [128, 512], BF16) for i in range(2)]
        stab = [sb("p1_stab%d" % i, [128, 512], BF16) for i in range(2)]
        qa_sb = [sb("p1_qa%d" % i, [128, 512], BF16) for i in range(2)]
        m1_sb = sb("p1_m1", [128, 512], F32)
        m2_sb = sb("p1_m2", [128, 512], F32)
        psq = [ps("p1_psq%d" % i, [128, 512], F32) for i in range(2)]
        psr = ps("p1_psr", [128, 512], F32)
        psv = [ps("p1_psv%d" % i, [128, 256], F32) for i in range(2)]

        p.wait("sync", "cc", vcc)
        p.wait("tensor", "gd", vwts)
        xg = ap["xln_gath"]

        xbuf_free = [0, 0]
        qa_free = [0, 0]
        psq_free = [0, 0]
        psr_free = 0
        psv_free = [0, 0]
        tab_free = [0, 0]
        tile_i = 0

        for tb in range(16):
            par = tb % 2
            buf = xbuf[par]
            r, jj = tb // 2, tb % 2
            p.wait("sync", "pe", xbuf_free[par])
            for kc in range(16):
                p.dma("sync", "sd", lambda e, b=buf, rr=r, k=kc, j2=jj: e.dma_start(
                    out=b[:, k * 512:(k + 1) * 512],
                    in_=xg[rr * D + k * 128: rr * D + (k + 1) * 128, j2 * 512:j2 * 512 + 512]))
            p.wait("sync", "dve", tab_free[par])
            p.dma("sync", "sd", lambda e, t=tb, pr=par: e.dma_start(
                out=ctab[pr][:], in_=ap["rope_cos"][:, t * 512:(t + 1) * 512]))
            vx = p.dma("sync", "sd", lambda e, t=tb, pr=par: e.dma_start(
                out=stab[pr][:], in_=ap["rope_sin"][:, t * 512:(t + 1) * 512]))
            p.wait("tensor", "sd", vx)

            for (w_sb, dst) in ((wq_sb, q_sb), (wk_sb, k_sb)):
                for hl in range(2):
                    ti = tile_i % 2
                    pq = psq[ti]
                    p.wait("tensor", "act", psq_free[ti])
                    for kc in range(15):
                        p.run("tensor", lambda e, w=w_sb, b=buf, k=kc, h=hl, q=pq: e.matmul(
                            q[:], w[:, k * 256 + h * 128:k * 256 + h * 128 + 128],
                            b[:, k * 512:(k + 1) * 512], start=(k == 0), stop=False))
                    vmm = p.do("tensor", "pe", lambda e, w=w_sb, b=buf, h=hl, q=pq: e.matmul(
                        q[:], w[:, 15 * 256 + h * 128:15 * 256 + h * 128 + 128],
                        b[:, 15 * 512:16 * 512], start=False, stop=True))
                    p.wait("scalar", "pe", vmm)
                    p.wait("scalar", "pe", qa_free[ti])
                    va = p.do("scalar", "act", lambda e, q=pq, t=ti: e.activation(
                        qa_sb[t][:], q[:], AF.Identity))
                    psq_free[ti] = va
                    p.wait("tensor", "act", va)
                    p.wait("tensor", "dve", psr_free)
                    vj = p.do("tensor", "pe", lambda e, t=ti: e.matmul(
                        psr[:], jt_sb[:], qa_sb[t][:], start=True, stop=True))
                    qa_free[ti] = vj
                    p.wait("vector", "act", va)
                    vm1 = p.do("vector", "dve", lambda e, t=ti, pr=par: e.tensor_mul(
                        m1_sb[:], qa_sb[t][:], ctab[pr][:]))
                    p.wait("vector", "pe", vj)
                    vm2 = p.do("vector", "dve", lambda e, pr=par: e.tensor_mul(
                        m2_sb[:], psr[:], stab[pr][:]))
                    psr_free = vm2
                    p.wait("vector", "dve", vm2)
                    vqf = p.do("vector", "dve", lambda e, d2=dst, h=hl, t=tb: e.tensor_add(
                        d2[:, h * S + t * 512:h * S + t * 512 + 512], m1_sb[:], m2_sb[:]))
                    tab_free[par] = vqf
                    tile_i += 1
            for ts in range(4):
                tv = ts % 2
                pv = psv[tv]
                p.wait("tensor", "act", psv_free[tv])
                for kc in range(15):
                    p.run("tensor", lambda e, b=buf, k=kc, s2=ts, q=pv: e.matmul(
                        q[:], b[:, k * 512 + s2 * 128:k * 512 + s2 * 128 + 128],
                        wv_sb[:, k * 256:(k + 1) * 256], start=(k == 0), stop=False))
                vmm = p.do("tensor", "pe", lambda e, b=buf, s2=ts, q=pv: e.matmul(
                    q[:], b[:, 15 * 512 + s2 * 128:15 * 512 + s2 * 128 + 128],
                    wv_sb[:, 15 * 256:16 * 256], start=False, stop=True))
                p.wait("scalar", "pe", vmm)
                gtile = tb * 4 + ts
                va = p.do("scalar", "act", lambda e, q=pv, g2=gtile: e.activation(
                    v_sb[:, g2 * 256:(g2 + 1) * 256], q[:], AF.Identity))
                psv_free[tv] = va
            xbuf_free[par] = p.val("pe")

        if "q_fm" in dbg:
            p.wait("sync", "dve", p.val("dve"))
            p.dma("sync", "sd", lambda e: e.dma_start(
                out=dbg_bufs["q_fm"].rearrange("(h p) t -> p h t", p=128), in_=q_sb[:]))
            p.dma("sync", "sd", lambda e: e.dma_start(
                out=dbg_bufs["k_fm"].rearrange("(h p) t -> p h t", p=128), in_=k_sb[:]))
            p.wait("sync", "act", p.val("act"))
            p.dma("sync", "sd", lambda e: e.dma_start(out=dbg_bufs["v_tm"], in_=v_sb[:]))

        sqb = st["contrib_sb"]
        nm4 = sb("p1d_nm", [1, 16], F32)
        p.wait("scalar", "dve", p.val("dve"))
        p1d_hist = []
        for h in range(2):
            for mi, mat in enumerate((q_sb, k_sb)):
                vsq = p.do("scalar", "act", lambda e, m=mat, h2=h: e.activation(
                    sqb[:], m[:, h2 * S:(h2 + 1) * S], AF.Square))
                p.wait("tensor", "act", vsq)
                for grp in range(2):
                    col = 4 * h + 2 * mi + grp
                    for chk in range(16):
                        pq = psq[len(p1d_hist) % 2]
                        if len(p1d_hist) >= 2:
                            p.wait("tensor", "dve", p1d_hist[-2])
                        vmm = p.do("tensor", "pe", lambda e, c2=chk, q=pq, g2=grp: e.matmul(
                            q[0:1, :], ones_col_bf[64 * g2:64 * g2 + 64, :],
                            sqb[64 * g2:64 * g2 + 64, c2 * 512:(c2 + 1) * 512],
                            start=True, stop=True))
                        p.wait("vector", "pe", vmm)
                        vcm = p.do("vector", "dve", lambda e, q=pq: e.tensor_reduce(
                            nm4[0:1, 8:9], q[0:1, :], AX.X, OP.max))
                        p.wait("vector", "dve", vcm)
                        if chk == 0:
                            vacc = p.do("vector", "dve", lambda e, c3=col: e.tensor_copy(
                                nm4[0:1, c3:c3 + 1], nm4[0:1, 8:9]))
                        else:
                            vacc = p.do("vector", "dve", lambda e, c3=col: e.tensor_max(
                                nm4[0:1, c3:c3 + 1], nm4[0:1, c3:c3 + 1], nm4[0:1, 8:9]))
                        p.wait("vector", "dve", vacc)
                        p1d_hist.append(vacc)
        for h in range(2):
            vm1 = p.do("vector", "dve", lambda e, h2=h: e.tensor_mul(
                nm4[0:1, 9:10], nm4[0:1, 4 * h2:4 * h2 + 1], nm4[0:1, 4 * h2 + 2:4 * h2 + 3]))
            p.wait("vector", "dve", vm1)
            vm2 = p.do("vector", "dve", lambda e, h2=h: e.tensor_mul(
                nm4[0:1, 10:11], nm4[0:1, 4 * h2 + 1:4 * h2 + 2], nm4[0:1, 4 * h2 + 3:4 * h2 + 4]))
            p.wait("vector", "dve", vm2)
            vmx = p.do("vector", "dve", lambda e: e.tensor_max(
                nm4[0:1, 9:10], nm4[0:1, 9:10], nm4[0:1, 10:11]))
            p.wait("scalar", "dve", vmx)
            vln = p.do("scalar", "act", lambda e: e.activation(nm4[0:1, 11:12], nm4[0:1, 9:10], AF.Ln))
            p.wait("scalar", "act", vln)
            vex = p.do("scalar", "act", lambda e: e.activation(nm4[0:1, 12:13], nm4[0:1, 11:12], AF.Exp, scale=0.5))
            p.wait("vector", "act", vex)
            vneg = p.do("vector", "dve", lambda e: e.tensor_scalar_mul(nm4[0:1, 9:10], nm4[0:1, 12:13], -1.0))
            p.wait("tensor", "dve", vneg)
            vbb = p.do("tensor", "pe", lambda e: e.matmul(
                psq[0][0:128, 0:1], ones_row_f[:], nm4[0:1, 9:10], start=True, stop=True))
            p.wait("vector", "pe", vbb)
            vmh = p.do("vector", "dve", lambda e, h2=h: e.tensor_copy(
                mhat_neg[:, h2:h2 + 1], psq[0][0:128, 0:1]))
            p.wait("tensor", "dve", vmh)
        if "mhat" in dbg:
            p.wait("sync", "dve", p.val("dve"))
            p.dma("sync", "sd", lambda e: e.dma_start(out=dbg_bufs["mhat"], in_=mhat_neg[:]))
        st["p1_end_dve"] = p.val("dve")
        st["p1_end_act"] = p.val("act")
        st["p1_end_pe"] = p.val("pe")
        grp1c.close()

    def phase2(grp):
        sb = lambda n, s_, d_: grp.enter_context(nc.sbuf_tensor(n, s_, d_))
        ps = lambda n, s_, d_: grp.enter_context(nc.psum_tensor(n, s_, d_))
        q_sb, k_sb, v_sb = st["q_sb"], st["k_sb"], st["v_sb"]
        contrib = st["contrib_sb"]

        S_ps = ps("p2_S", [128, 2048], F32)
        u1 = ps("p2_u1", [128, 512], F32)
        u2 = ps("p2_u2", [128, 512], F32)
        lp = ps("p2_lp", [128, 512], F32)
        fin = ps("p2_fin", [128, 512], F32)
        p_sb = sb("p2_p", [128, 2048], BF16)
        l1row = sb("p2_l1", [1, 512], F32)
        l2row = sb("p2_l2", [1, 512], F32)
        r1 = sb("p2_r1", [1, 512], F32)
        r2 = sb("p2_r2", [1, 512], F32)
        r3 = sb("p2_r3", [1, 512], F32)
        crow = sb("p2_crow", [1, 512], F32)
        m1row = sb("p2_m1r", [1, 512], F32)
        rstdrow = sb("p2_rstd", [1, 512], F32)
        cb_sb = sb("p2_cb", [128, 512], F32)
        rb_sb = sb("p2_rb", [128, 512], F32)
        tt_sb = sb("p2_tt", [128, 512], F32)
        d_sb = sb("p2_d", [128, 512], F32)
        sq2_sb = sb("p2_sq2", [128, 512], F32)

        p.wait("tensor", "dve", st["p1_end_dve"])
        p.wait("tensor", "act", st["p1_end_act"])
        p.wait("scalar", "dve", st["p1_end_dve"])

        v_exp = {}
        v_s = {}
        v_av = {}
        v_mask = {}
        v_f9 = 0
        v_lev = 0
        v_f15_prev = 0
        v_f17_last = 0
        v_cdma = 0
        m = 0
        pend = None

        def emit_av(desc):
            (mm, h2, b2, ch2, nch2, t0_) = desc
            par2 = mm % 2
            diag = ch2 >= 4 * b2
            p.wait("tensor", "act", v_exp[mm])
            if diag:
                p.wait("tensor", "dve", v_mask[mm])
            if ch2 == 0:
                p.wait("tensor", "dve", v_f9)
                p.wait("tensor", "dve", v_lev)
                p.wait("tensor", "dve", v_f15_prev)
            first = (ch2 == 0)
            last = (ch2 == nch2 - 1)
            p.run("tensor", lambda e, c3=ch2, h3=h2, pr=par2, f=first, l=last: e.matmul(
                u1[:], v_sb[:, c3 * 256 + h3 * 128:c3 * 256 + h3 * 128 + 128],
                p_sb[:, pr * 1024:pr * 1024 + 512], start=f, stop=l))
            p.run("tensor", lambda e, c3=ch2, h3=h2, pr=par2, f=first, l=last: e.matmul(
                u2[:], v_sb[:, c3 * 256 + h3 * 128:c3 * 256 + h3 * 128 + 128],
                p_sb[:, pr * 1024 + 512:pr * 1024 + 1024], start=f, stop=l))
            p.run("tensor", lambda e, pr=par2, f=first, l=last: e.matmul(
                lp[0:1, :], ones_col_bf[:], p_sb[:, pr * 1024:pr * 1024 + 512],
                start=f, stop=l, skip_group_check=True))
            v = p.do("tensor", "pe", lambda e, pr=par2, f=first, l=last: e.matmul(
                fin[0:1, :], ones_col_bf[:], p_sb[:, pr * 1024 + 512:pr * 1024 + 1024],
                start=f, stop=l, skip_group_check=True))
            v_av[mm] = v

        for h in range(2):
            for b in range(16):
                t0 = 512 * b
                nch = 4 * (b + 1)
                for ch in range(nch):
                    par = m % 2
                    if P2SER and m >= 1:
                        p.wait("tensor", "pe", v_av.get(m - 1, 0))
                    if m >= 2:
                        p.wait("tensor", "act", v_exp[m - 2])
                    p.run("tensor", lambda e, h2=h, c2=ch, pr=par, tq=t0: e.matmul(
                        S_ps[:, pr * 1024:pr * 1024 + 512],
                        k_sb[0:64, h2 * S + c2 * 128:h2 * S + c2 * 128 + 128],
                        q_sb[0:64, h2 * S + tq:h2 * S + tq + 512], start=True, stop=True))
                    v_s[m] = p.do("tensor", "pe", lambda e, h2=h, c2=ch, pr=par, tq=t0: e.matmul(
                        S_ps[:, pr * 1024 + 512:pr * 1024 + 1024],
                        k_sb[64:128, h2 * S + c2 * 128:h2 * S + c2 * 128 + 128],
                        q_sb[64:128, h2 * S + tq:h2 * S + tq + 512], start=True, stop=True))
                    p.wait("scalar", "pe", v_s[m])
                    if m >= 2:
                        p.wait("scalar", "pe", v_av[m - 2])
                    v_exp[m] = p.do("scalar", "act", lambda e, pr=par, h2=h: e.activation(
                        p_sb[:, pr * 1024:pr * 1024 + 1024],
                        S_ps[:, pr * 1024:pr * 1024 + 1024],
                        AF.Exp, bias=mhat_neg[:, h2:h2 + 1]))
                    if ch >= 4 * b:
                        a = ch - 4 * b
                        p.wait("vector", "act", v_exp[m])
                        v_mask[m] = p.do("vector", "dve", lambda e, pr=par, a2=a: e.tensor_mul(
                            p_sb[:, pr * 1024:pr * 1024 + 1024],
                            p_sb[:, pr * 1024:pr * 1024 + 1024],
                            mask_sb[0][:, a2 * 1024:(a2 + 1) * 1024]))
                    if pend is not None:
                        emit_av(pend)
                    pend = (m, h, b, ch, nch, t0)
                    m += 1
                emit_av(pend)
                pend = None
                blast = v_av[m - 1]
                p.wait("vector", "pe", blast)
                p.do("vector", "dve", lambda e: e.tensor_scalar_mul(l1row[:], lp[0:1, :], KSC))
                v_lev = p.do("vector", "dve", lambda e: e.tensor_scalar_mul(l2row[:], fin[0:1, :], KSC))
                p.wait("scalar", "dve", v_lev)
                p.do("scalar", "act", lambda e: e.activation(r1[:], l1row[:], AF.Ln))
                vf2 = p.do("scalar", "act", lambda e: e.activation(r2[:], l2row[:], AF.Ln))
                p.wait("vector", "act", vf2)
                vf3 = p.do("vector", "dve", lambda e: e.tensor_sub(r3[:], r1[:], r2[:]))
                p.wait("scalar", "dve", vf3)
                vf4 = p.do("scalar", "act", lambda e: e.activation(crow[:], r3[:], AF.Exp))
                p.wait("vector", "act", vf4)
                vf5 = p.do("vector", "dve", lambda e: e.tensor_single_scalar(
                    crow[:], crow[:], lam_sb[0:1, 0:1], OP.mult))
                p.wait("tensor", "dve", vf5)
                vf6 = p.do("tensor", "pe", lambda e: e.matmul(
                    fin[:], ones_row_f[:], crow[:], start=True, stop=True))
                p.wait("vector", "pe", vf6)
                vf7 = p.do("vector", "dve", lambda e: e.tensor_copy(cb_sb[:], fin[:]))
                p.wait("vector", "dve", vf7)
                vf8 = p.do("vector", "dve", lambda e: e.scalar_tensor_tensor(
                    tt_sb[:], u2[:], KSC, cb_sb[:], OP.mult, OP.mult))
                p.wait("vector", "dve", vf8)
                v_f9 = p.do("vector", "dve", lambda e: e.scalar_tensor_tensor(
                    d_sb[:], u1[:], KSC, tt_sb[:], OP.mult, OP.subtract))
                p.wait("vector", "dve", v_f9)
                vf10 = p.do("vector", "dve", lambda e: e.tensor_mul(sq2_sb[:], d_sb[:], d_sb[:]))
                p.wait("tensor", "dve", vf10)
                vf11 = p.do("tensor", "pe", lambda e: e.matmul(
                    fin[0:1, :], ones_col_f[:], sq2_sb[:], start=True, stop=True))
                p.wait("scalar", "pe", vf11)
                vf12 = p.do("scalar", "act", lambda e: e.activation(
                    m1row[:], fin[0:1, :], AF.Ln, scale=1.0 / 128))
                p.wait("scalar", "act", vf12)
                vf13 = p.do("scalar", "act", lambda e: e.activation(
                    rstdrow[:], m1row[:], AF.Exp, scale=-0.5))
                p.wait("tensor", "act", vf13)
                vf14 = p.do("tensor", "pe", lambda e: e.matmul(
                    fin[:], ones_row_f[:], rstdrow[:], start=True, stop=True))
                p.wait("vector", "pe", vf14)
                v_f15_prev = p.do("vector", "dve", lambda e: e.tensor_copy(rb_sb[:], fin[:]))
                p.wait("vector", "dve", v_f15_prev)
                vf16 = p.do("vector", "dve", lambda e: e.tensor_mul(tt_sb[:], d_sb[:], rb_sb[:]))
                if "lrows" in dbg and h == 0:
                    p.wait("sync", "dve", vf5)
                    p.dma("sync", "sd", lambda e, b2=b: e.dma_start(
                        out=dbg_bufs["lrows"][3 * b2:3 * b2 + 1, :], in_=l1row[:]))
                    p.dma("sync", "sd", lambda e, b2=b: e.dma_start(
                        out=dbg_bufs["lrows"][3 * b2 + 1:3 * b2 + 2, :], in_=l2row[:]))
                    vdd = p.dma("sync", "sd", lambda e, b2=b: e.dma_start(
                        out=dbg_bufs["lrows"][3 * b2 + 2:3 * b2 + 3, :], in_=crow[:]))
                    p.wait("vector", "sd", vdd)
                p.wait("scalar", "dve", vf16)
                if v_cdma:
                    p.wait("scalar", "sd", v_cdma)
                v_f17_last = p.do("scalar", "act", lambda e, tq=t0: e.activation(
                    contrib[:, tq:tq + 512], tt_sb[:], AF.Identity,
                    scale=lnw_eff[:], bias=lnb_eff[:]))
            p.wait("sync", "act", v_f17_last)
            acv = ap["attn_contrib"].rearrange("(j d) t -> j d t", d=256)
            v_cdma = p.dma("sync", "sd", lambda e, h2=h: e.dma_start(
                out=acv[:, h2 * 128:h2 * 128 + 128, :].rearrange("j d t -> d j t"),
                in_=contrib[:].rearrange("p (j t) -> p j t", j=8)))
        p.wait("gpsimd", "sd", p.val("sd"))
        def _a2a(e):
            with allow_ici_all_to_all():
                return e.collective_compute(
                    "AllToAll", OP.bypass, replica_groups=[list(range(NCORE))],
                    ins=[ap["attn_contrib"].opt()], outs=[ap["attn_slice"].opt()])
        vcc2 = p.do("gpsimd", "cc", _a2a)
        st["a2a"] = vcc2
        if "contrib" in dbg:
            p.wait("sync", "cc", vcc2)
            p.dma("sync", "sd", lambda e: e.dma_start(
                out=dbg_bufs["contrib"], in_=ap["attn_slice"]))

    def phase3(grp):
        p3sb = ExitStack()
        sb = lambda n, s_, d_: p3sb.enter_context(nc.sbuf_tensor(n, s_, d_))
        p3ps = ExitStack()
        ps = lambda n, s_, d_: p3ps.enter_context(nc.psum_tensor(n, s_, d_))

        sqs3 = grp.enter_context(nc.sbuf_tensor("p3_sqs", [128, D], F32))
        st3 = grp.enter_context(nc.sbuf_tensor("p3_st", [128, 8], F32))
        st["sqs3"] = sqs3; st["st3"] = st3
        st["ysb_bf"] = grp.enter_context(nc.sbuf_tensor("p4_ybf", [128, 8 * 1024], BF16))
        st["stack_xlat"] = ExitStack()
        st["xlat"] = st["stack_xlat"].enter_context(nc.sbuf_tensor("p4_xlat", [128, 8 * 1024], BF16))
        st["stack_x3"] = ExitStack()
        xln3_fm = st["stack_x3"].enter_context(nc.sbuf_tensor("p3_xh", [128, 16 * 1024], BF16))
        st["xln3_fm"] = xln3_fm
        st["stack_xlo"] = ExitStack()
        xlo_fm = st["stack_xlo"].enter_context(nc.sbuf_tensor("p3_xl", [128, 16 * 1024], BF16))
        st["xlo_fm"] = xlo_fm
        wo_sb = sb("p3_wo", [128, 16 * 2048], BF16)
        asb = [sb("p3_asb%d" % i, [128, 16 * 128], BF16) for i in range(2)]
        xres = [sb("p3_xr0", [128, D], F32)] * 2
        x2t = sb("p3_x2", [128, D], F32)
        xln3_f = sqs3
        x3tm = sb("p3_x3tm", [128, D], BF16)
        xltm = sb("p3_xltm", [128, D], BF16)
        po = ps("p3_po", [128, 2048], F32)
        tp2 = [ps("p3_tp%d" % i, [128, 128], BF16) for i in range(2)]

        p.wait("sync", "cc", st["a2a"])
        p.wait("gpsimd", "cc", st["a2a"])
        p.wait("tensor", "cc", st["a2a"])
        v_wo = p.dma("gpsimd", "gd", lambda e: e.dma_start(
            out=wo_sb[:], in_=ap["wo"].rearrange("(k p) m -> p k m", p=128)))
        p.wait("tensor", "gd", v_wo)
        asb_free = [0, 0]

        def ln_stats(src_ap, out_sc, out_nb, eng_wait_act, eng_wait_dve, src_psum):
            vs1 = p.do("vector", "dve", lambda e: e.tensor_reduce(
                st3[:, 0:1], src_ap, AX.X, OP.add))
            p.wait("scalar", "dve", vs1)
            vs2 = p.do("scalar", "act", lambda e: e.activation(sqs3[:], src_ap, AF.Square))
            p.wait("vector", "act", vs2)
            vs2b = p.do("vector", "dve", lambda e: e.tensor_reduce(
                st3[:, 1:2], sqs3[:], AX.X, OP.add))
            p.wait("vector", "dve", vs2b)
            vmu = p.do("vector", "dve", lambda e: e.tensor_scalar_mul(st3[:, 2:3], st3[:, 0:1], 1.0 / D))
            p.wait("vector", "dve", vmu)
            vw = p.do("vector", "dve", lambda e: e.tensor_mul(st3[:, 3:4], st3[:, 2:3], st3[:, 2:3]))
            p.wait("vector", "dve", vw)
            vv = p.do("vector", "dve", lambda e: e.scalar_tensor_tensor(
                st3[:, 4:5], st3[:, 1:2], 1.0 / D, st3[:, 3:4], OP.mult, OP.subtract))
            p.wait("scalar", "dve", vv)
            vl = p.do("scalar", "act", lambda e: e.activation(st3[:, 5:6], st3[:, 4:5], AF.Ln, bias=eps_ln_sb[:]))
            p.wait("scalar", "act", vl)
            vr = p.do("scalar", "act", lambda e, o=out_sc: e.activation(st3[:, o:o + 1], st3[:, 5:6], AF.Exp, scale=-0.5))
            p.wait("vector", "act", vr)
            vn = p.do("vector", "dve", lambda e, o=out_sc, o2=out_nb: e.scalar_tensor_tensor(
                st3[:, o2:o2 + 1], st3[:, 2:3], -1.0, st3[:, o:o + 1], OP.mult, OP.mult))
            return vr, vn

        tp_hist = []
        v_scr_free = 0
        v_x2_free = [0, 0]
        for tc in range(P3N):
            xrb = xres[tc % 2]
            p.wait("sync", "dve", v_x2_free[tc % 2])
            v_xr = p.dma("sync", "sd", lambda e, b=xrb, t=tc: e.dma_start(
                out=b[:], in_=ap["x_slice"][t * 128:(t + 1) * 128, :]))
            ab = asb[tc % 2]
            p.wait("sync", "pe", asb_free[tc % 2])
            for kc in range(16):
                v_as = p.dma("sync", "sd", lambda e, k=kc, t=tc, b=ab: e.dma_start(
                    out=b[:, k * 128:(k + 1) * 128],
                    in_=ap["attn_slice"][k * 128:(k + 1) * 128, t * 128:(t + 1) * 128]))
            p.wait("tensor", "sd", v_as)
            for mc in range(4):
                for kc in range(15):
                    p.run("tensor", lambda e, k=kc, m2=mc, b=ab: e.matmul(
                        po[:, m2 * 512:m2 * 512 + 512],
                        b[:, k * 128:(k + 1) * 128],
                        wo_sb[:, k * 2048 + m2 * 512:k * 2048 + m2 * 512 + 512],
                        start=(k == 0), stop=False))
                vmm = p.do("tensor", "pe", lambda e, m2=mc, b=ab: e.matmul(
                    po[:, m2 * 512:m2 * 512 + 512],
                    b[:, 15 * 128:16 * 128],
                    wo_sb[:, 15 * 2048 + m2 * 512:15 * 2048 + m2 * 512 + 512],
                    start=False, stop=True))
            asb_free[tc % 2] = vmm
            p.wait("vector", "pe", vmm)
            p.wait("scalar", "pe", vmm)
            p.wait("scalar", "act", v_scr_free)
            r2, n2 = ln_stats(po[:, :], 6, 7, 0, 0, True)
            p.wait("scalar", "dve", n2)
            va2 = p.do("scalar", "act", lambda e: e.activation(
                x2t[:], po[:, :], AF.Identity, bias=st3[:, 7:8], scale=st3[:, 6:7]))
            p.wait("vector", "act", va2)
            p.wait("vector", "sd", v_xr)
            vx2 = p.do("vector", "dve", lambda e, b=xrb: e.tensor_add(x2t[:], x2t[:], b[:]))
            v_x2_free[tc % 2] = vx2
            p.wait("sync", "dve", vx2)
            p.dma("sync", "sd", lambda e, t=tc: e.dma_start(
                out=ap["x2_park"][t * 128:(t + 1) * 128, :], in_=x2t[:]))
            p.wait("vector", "dve", vx2)
            r3, n3 = ln_stats(x2t[:], 6, 7, 0, 0, False)
            p.wait("scalar", "dve", n3)
            va3 = p.do("scalar", "act", lambda e: e.activation(
                x3tm[:], x2t[:], AF.Identity, bias=st3[:, 7:8], scale=st3[:, 6:7]))
            p.wait("scalar", "act", va3)
            va3f = p.do("scalar", "act", lambda e: e.activation(
                xln3_f[:], x2t[:], AF.Identity, bias=st3[:, 7:8], scale=st3[:, 6:7]))
            p.wait("vector", "act", va3f)
            vlo = p.do("vector", "dve", lambda e: e.tensor_sub(xltm[:], xln3_f[:], x3tm[:]))
            v_scr_free = va3f
            p.wait("tensor", "act", va3)
            p.wait("tensor", "dve", vlo)
            for (src, dstf) in ((x3tm, xln3_fm), (xltm, xlo_fm)):
                for kc in range(16):
                    slot = len(tp_hist) % 2
                    pslot = tp2[slot]
                    if len(tp_hist) >= 2:
                        p.wait("tensor", "dve", tp_hist[len(tp_hist) - 2])
                    vt = p.do("tensor", "pe", lambda e, s2=pslot, k=kc, sr=src: e.transpose(
                        s2[:], sr[:, k * 128:(k + 1) * 128], iden_bf[:]))
                    p.wait("vector", "pe", vt)
                    ve = p.do("vector", "dve", lambda e, s2=pslot, k=kc, t=tc, df=dstf: e.tensor_copy(
                        df[:, k * TC + t * 128:k * TC + t * 128 + 128], s2[:]))
                    tp_hist.append(ve)
        st["p3_pe"] = p.val("pe")
        st["p3_dve"] = p.val("dve")
        st["p3_act"] = p.val("act")
        p3ps.close()
        p3sb.close()
        if "x2" in dbg:
            p.wait("sync", "dve", p.val("dve"))
            p.dma("sync", "sd", lambda e: e.dma_start(out=dbg_bufs["x2"], in_=ap["x2_park"]))
        if "xln3" in dbg:
            p.wait("sync", "dve", p.val("dve"))
            p.dma("sync", "sd", lambda e: e.dma_start(
                out=dbg_bufs["xln3"].rearrange("(k p) t -> p k t", p=128), in_=xln3_fm[:]))

    def phase4a(grp):
        sb = lambda n, s_, d_: grp.enter_context(nc.sbuf_tensor(n, s_, d_))
        p4ps = ExitStack()
        ps = lambda n, s_, d_: p4ps.enter_context(nc.psum_tensor(n, s_, d_))
        xln3_fm, xlo_fm = st["xln3_fm"], st["xlo_fm"]

        g4a = ExitStack()
        sbg = lambda n, s_, d_: g4a.enter_context(nc.sbuf_tensor(n, s_, d_))
        gfull = sbg("p4_gf", [128, 16 * NE], F32)
        ghi = sbg("p4_ghi", [128, 16 * NE], BF16)
        glo = sbg("p4_glo", [128, 16 * NE], BF16)
        e1 = sbg("p4_e1", [128, NE], F32)
        s_sb = sbg("p4_s", [128, NE], F32)
        sbv = sbg("p4_sbv", [128, NE], F32)
        gm = sbg("p4_gm", [128, G], F32)
        gsx = sbg("p4_gsx", [128, NE], F32)
        gs = sbg("p4_gs", [128, G], F32)
        g12 = sbg("p4_g12", [128, 2], F32)
        sel = sbg("p4_sel", [128, NE], F32)
        msk = sbg("p4_msk", [128, NE], F32)
        wsel = sbg("p4_wsel", [128, NE], F32)
        wcol = sbg("p4_wcol", [128, 2], F32)
        cw_all = sbg("p4_cw", [128, 8 * NE], F32)
        cwt_st = sbg("p4_cwt", [32, 128], F32)
        psg = ps("p4_psg", [128, 512], F32)
        psx = [ps("p4_psx%d" % i, [128, 512], F32) for i in range(2)]
        pst4 = ps("p4_pst", [128, 128], F32)

        p.wait("sync", "act", st["p3_act"])
        v_gf = p.dma("sync", "sd", lambda e: e.dma_start(
            out=gfull[:], in_=ap["gate_wT"].rearrange("(k p) e -> p k e", p=128)))
        p.wait("vector", "sd", v_gf)
        vhi = p.do("vector", "dve", lambda e: e.tensor_copy(ghi[:], gfull[:]))
        p.wait("vector", "dve", vhi)
        vlo = p.do("vector", "dve", lambda e: e.tensor_sub(glo[:], gfull[:], ghi[:]))
        p.wait("tensor", "dve", vlo)
        p.wait("tensor", "dve", st["p3_dve"])
        p.wait("tensor", "act", st["p3_act"])

        g3 = lambda a: a.rearrange("p (g e) -> p g e", g=G)
        v_cwt_dma = 0
        for tc in range(8):
            combos = [(xf, gw, kc) for (xf, gw) in ((xln3_fm, ghi), (xln3_fm, glo), (xlo_fm, ghi))
                      for kc in range(16)]
            for ci, (xf, gw, kc) in enumerate(combos[:-1]):
                p.run("tensor", lambda e, x=xf, g2=gw, k=kc, t=tc, fr=(ci == 0): e.matmul(
                    psg[:, 0:NE], x[:, k * TC + t * 128:k * TC + t * 128 + 128],
                    g2[:, k * NE:(k + 1) * NE], start=fr, stop=False))
            xf, gw, kc = combos[-1]
            vg = p.do("tensor", "pe", lambda e, x=xf, g2=gw, k=kc, t=tc: e.matmul(
                psg[:, 0:NE], x[:, k * TC + t * 128:k * TC + t * 128 + 128],
                g2[:, k * NE:(k + 1) * NE], start=False, stop=True))
            p.wait("scalar", "pe", vg)
            ve1 = p.do("scalar", "act", lambda e: e.activation(e1[:], psg[:, 0:NE], AF.Exp, scale=-1.0))
            p.wait("vector", "act", ve1)
            v1 = p.do("vector", "dve", lambda e: e.tensor_scalar_add(e1[:], e1[:], 1.0))
            p.wait("vector", "dve", v1)
            v2 = p.do("vector", "dve", lambda e: e.reciprocal(s_sb[:], e1[:]))
            p.wait("vector", "dve", v2)
            v3 = p.do("vector", "dve", lambda e: e.tensor_add(sbv[:], s_sb[:], gb_sb[:]))
            p.wait("vector", "dve", v3)
            v4 = p.do("vector", "dve", lambda e: e.tensor_reduce(gm[:], g3(sbv[:]), AX.X, OP.max))
            p.wait("vector", "dve", v4)
            gmb = gm[:].rearrange("p (g one) -> p g one", one=1).broadcast_to([128, G, NE // G])
            v5 = p.do("vector", "dve", lambda e: e.tensor_tensor(g3(msk[:]), g3(sbv[:]), gmb, OP.is_ge))
            p.wait("vector", "dve", v5)
            v6 = p.do("vector", "dve", lambda e: e.scalar_tensor_tensor(
                gsx[:], msk[:], -1e30, sbv[:], OP.mult, OP.add))
            p.wait("vector", "dve", v6)
            v7 = p.do("vector", "dve", lambda e: e.tensor_reduce(gs[:], g3(gsx[:]), AX.X, OP.max))
            p.wait("vector", "dve", v7)
            v8 = p.do("vector", "dve", lambda e: e.tensor_add(gs[:], gs[:], gm[:]))
            p.wait("vector", "dve", v8)
            g1v = g12[:, 0:1]
            v9 = p.do("vector", "dve", lambda e: e.tensor_reduce(
                g1v.rearrange("p (a one) -> p a one", one=1),
                gs[:].rearrange("p (a g) -> p a g", a=1), AX.X, OP.max))
            p.wait("vector", "dve", v9)
            g1b = g12[:, 0:1].rearrange("p (a one) -> p a one", one=1).broadcast_to([128, 1, G])
            v10 = p.do("vector", "dve", lambda e: e.tensor_tensor(
                msk[:, 0:G].rearrange("p (a g) -> p a g", a=1),
                gs[:].rearrange("p (a g) -> p a g", a=1), g1b, OP.is_ge))
            p.wait("vector", "dve", v10)
            v11 = p.do("vector", "dve", lambda e: e.scalar_tensor_tensor(
                gsx[:, 0:G], msk[:, 0:G], -1e30, gs[:], OP.mult, OP.add))
            p.wait("vector", "dve", v11)
            v12 = p.do("vector", "dve", lambda e: e.tensor_reduce(
                g12[:, 1:2].rearrange("p (a one) -> p a one", one=1),
                gsx[:, 0:G].rearrange("p (a g) -> p a g", a=1), AX.X, OP.max))
            p.wait("vector", "dve", v12)
            g2b = g12[:, 1:2].rearrange("p (a one) -> p a one", one=1).broadcast_to([128, 1, G])
            v13 = p.do("vector", "dve", lambda e: e.tensor_tensor(
                gm[:].rearrange("p (a g) -> p a g", a=1),
                gs[:].rearrange("p (a g) -> p a g", a=1), g2b, OP.is_ge))
            p.wait("vector", "dve", v13)
            v14 = p.do("vector", "dve", lambda e: e.tensor_scalar(
                msk[:, 0:G], gm[:], 1e30, 1e30, OP.mult, OP.subtract))
            p.wait("vector", "dve", v14)
            gmaskb = msk[:, 0:G].rearrange("p (g one) -> p g one", one=1).broadcast_to([128, G, NE // G])
            v15 = p.do("vector", "dve", lambda e: e.tensor_tensor(
                g3(gsx[:]), g3(sbv[:]), gmaskb, OP.add))
            p.wait("vector", "dve", v15)
            v16 = p.do("vector", "dve", lambda e: e.memset(sel[:], 0.0))
            vprev = v16
            for it in range(TOP_K):
                p.wait("vector", "dve", vprev)
                va_ = p.do("vector", "dve", lambda e: e.tensor_reduce(
                    wcol[:, 0:1].rearrange("p (a one) -> p a one", one=1),
                    gsx[:].rearrange("p (a g) -> p a g", a=1), AX.X, OP.max))
                p.wait("vector", "dve", va_)
                cmb = wcol[:, 0:1].rearrange("p (a one) -> p a one", one=1).broadcast_to([128, 1, NE])
                vb_ = p.do("vector", "dve", lambda e: e.tensor_tensor(
                    msk[:].rearrange("p (a g) -> p a g", a=1),
                    gsx[:].rearrange("p (a g) -> p a g", a=1), cmb, OP.is_ge))
                p.wait("vector", "dve", vb_)
                vc_ = p.do("vector", "dve", lambda e: e.tensor_add(sel[:], sel[:], msk[:]))
                vd_ = p.do("vector", "dve", lambda e: e.scalar_tensor_tensor(
                    gsx[:], msk[:], -1e30, gsx[:], OP.mult, OP.add))
                p.wait("vector", "dve", vd_)
                vprev = vd_
            vw1 = p.do("vector", "dve", lambda e: e.tensor_mul(wsel[:], sel[:], s_sb[:]))
            p.wait("vector", "dve", vw1)
            vw2 = p.do("vector", "dve", lambda e: e.tensor_reduce(
                wcol[:, 0:1].rearrange("p (a one) -> p a one", one=1),
                wsel[:].rearrange("p (a g) -> p a g", a=1), AX.X, OP.add))
            p.wait("vector", "dve", vw2)
            vw3 = p.do("vector", "dve", lambda e: e.reciprocal(wcol[:, 1:2], wcol[:, 0:1]))
            p.wait("vector", "dve", vw3)
            vw4 = p.do("vector", "dve", lambda e, t=tc: e.tensor_scalar(
                cw_all[:, t * NE:(t + 1) * NE], wsel[:], wcol[:, 1:2], RSCALE, OP.mult, OP.mult))
            p.wait("tensor", "dve", vw4)
            vt4 = p.do("tensor", "pe", lambda e, t=tc: e.transpose(
                pst4[0:32, :], cw_all[:, t * NE:(t + 1) * NE], iden_f[:]))
            p.wait("vector", "pe", vt4)
            vev = p.do("vector", "dve", lambda e: e.tensor_copy(cwt_st[:], pst4[0:32, :]))
            p.wait("sync", "dve", vev)
            v_cwt_dma = p.dma("sync", "sd", lambda e, t=tc: e.dma_start(
                out=ap["cw_park"][:, t * 128:(t + 1) * 128], in_=cwt_st[:]))
            p.wait("vector", "sd", v_cwt_dma)
            p.wait("tensor", "sd", v_cwt_dma)
        st["cw_dma"] = v_cwt_dma
        g4a.close()
        st["stack_xlo"].close()

        fc1stk = ExitStack()
        fc1sb = fc1stk.enter_context(nc.sbuf_tensor("p4_fc1", [128, 16 * 1024], BF16))
        xlat = st["xlat"]
        p.wait("gpsimd", "pe", p.val("pe"))
        p.wait("gpsimd", "dve", p.val("dve"))
        p.wait("gpsimd", "sd", p.val("sd"))
        v_f1 = p.dma("gpsimd", "gd", lambda e: e.dma_start(
            out=fc1sb[:], in_=ap["fc1T"].rearrange("(k p) l -> p k l", p=128)))
        p.wait("tensor", "gd", v_f1)
        psx_free = [0, 0]
        for lc in range(8):
            for th in range(2):
                i = (lc * 2 + th) % 2
                px = psx[i]
                p.wait("tensor", "act", psx_free[i])
                for kc in range(15):
                    p.run("tensor", lambda e, k=kc, l=lc, t2=th, q=px: e.matmul(
                        q[:], fc1sb[:, k * 1024 + l * 128:k * 1024 + l * 128 + 128],
                        xln3_fm[:, k * TC + t2 * 512:k * TC + t2 * 512 + 512],
                        start=(k == 0), stop=False))
                vmm = p.do("tensor", "pe", lambda e, l=lc, t2=th, q=px: e.matmul(
                    q[:], fc1sb[:, 15 * 1024 + l * 128:15 * 1024 + l * 128 + 128],
                    xln3_fm[:, 15 * TC + t2 * 512:15 * TC + t2 * 512 + 512],
                    start=False, stop=True))
                p.wait("scalar", "pe", vmm)
                va = p.do("scalar", "act", lambda e, l=lc, t2=th, q=px: e.activation(
                    xlat[:, l * 1024 + t2 * 512:l * 1024 + t2 * 512 + 512], q[:], AF.Identity))
                psx_free[i] = va
        p.wait("sync", "dve", st["p3_dve"])
        v_park = p.dma("sync", "sd", lambda e: e.dma_start(
            out=ap["xln3_park"].rearrange("(k p) t -> p k t", p=128), in_=xln3_fm[:]))
        st["xln3_park_dma"] = v_park
        st["p4a_act"] = p.val("act")
        st["p4a_pe"] = p.val("pe")
        st["p4a_dve"] = p.val("dve")
        p4ps.close()
        fc1stk.close()
        st["stack_x3"].close()
        if "cw" in dbg:
            p.wait("sync", "sd", st["cw_dma"])
            p.dma("sync", "sd", lambda e: e.dma_start(out=dbg_bufs["cw"], in_=ap["cw_park"]))
        if "xlat" in dbg:
            p.wait("sync", "act", p.val("act"))
            p.dma("sync", "sd", lambda e: e.dma_start(out=dbg_bufs["xlat"], in_=xlat[:]))

    def phase4b(grp):
        p4sb = ExitStack()
        sb = lambda n, s_, d_: p4sb.enter_context(nc.sbuf_tensor(n, s_, d_))
        p4ps = ExitStack()
        ps = lambda n, s_, d_: p4ps.enter_context(nc.psum_tensor(n, s_, d_))
        xlat = st["xlat"]

        ysb_bf = st["ysb_bf"]
        w1sb = [sb("p4_w1%d" % i, [128, 8 * 1024], BF16) for i in range(2)]
        w2sb = [sb("p4_w2%d" % i, [128, 8 * 1024], BF16) for i in range(2)]
        hbuf = [sb("p4_h%d" % i, [128, 8 * 1024], BF16) for i in range(2)]
        y_sb = sb("p4_y", [128, 8 * 1024], F32)
        st["y_sb"] = y_sb
        crow = sb("p4_crow", [1, 2 * 1024], F32)
        cwb = [sb("p4_cwb%d" % i, [128, 1024], F32) for i in range(2)]
        t1s = sb("p4_t1", [128, 512], F32)
        u_s = sb("p4_u", [128, 512], F32)
        psh = [ps("p4_ph%d" % i, [128, 512], F32) for i in range(2)]
        psy = [ps("p4_py%d" % i, [128, 512], F32) for i in range(2)]
        pcb = ps("p4_pcb", [128, 512], F32)

        p.wait("tensor", "act", st["p4a_act"])
        p.wait("tensor", "dve", st["p4a_dve"])
        p.wait("sync", "sd", st["cw_dma"])
        p.wait("gpsimd", "pe", st["p4a_pe"])
        p.wait("gpsimd", "sd", st["xln3_park_dma"])

        w_free = [0, 0]
        crow_free = [0, 0]
        cwb_free = [0, 0]
        h_free = [0, 0]
        psh_free = [0, 0]
        psy_free = [0, 0]
        pcb_free = 0
        vdma_w = [0, 0]
        vdma_c = [0, 0]
        for e_ in range(NE):
            par = e_ % 2
            p.wait("gpsimd", "pe", w_free[par])
            p.dma("gpsimd", "gd", lambda ee, x=e_, pr=par: ee.dma_start(
                out=w1sb[pr][:], in_=ap["w1T"][x].rearrange("(lc p) i -> p lc i", p=128)))
            vdma_w[par] = p.dma("gpsimd", "gd", lambda ee, x=e_, pr=par: ee.dma_start(
                out=w2sb[pr][:], in_=ap["w2T"][x].rearrange("(ic p) l -> p ic l", p=128)))
            p.wait("sync", "pe", crow_free[par])
            vdma_c[par] = p.dma("sync", "sd", lambda ee, x=e_, pr=par: ee.dma_start(
                out=crow[0:1, pr * 1024:(pr + 1) * 1024], in_=ap["cw_park"][x:x + 1, :]))
            p.wait("tensor", "sd", vdma_c[par])
            p.wait("tensor", "dve", pcb_free)
            for th in range(2):
                vbb = p.do("tensor", "pe", lambda ee, pr=par, t2=th: ee.matmul(
                    pcb[:], ones_row_f[:], crow[0:1, pr * 1024 + t2 * 512:pr * 1024 + t2 * 512 + 512],
                    start=True, stop=True))
                crow_free[par] = vbb
                p.wait("vector", "pe", vbb)
                p.wait("vector", "dve", cwb_free[par] if th == 0 else 0)
                pcb_free = p.do("vector", "dve", lambda ee, pr=par, t2=th: ee.tensor_copy(
                    cwb[pr][:, t2 * 512:t2 * 512 + 512], pcb[:]))
                p.wait("tensor", "dve", pcb_free)
            p.wait("tensor", "gd", vdma_w[par])
            p.wait("vector", "pe", h_free[par])
            vh_last_dve = 0
            for ic in range(8):
                for th in range(2):
                    hp = (ic * 2 + th) % 2
                    ph = psh[hp]
                    p.wait("tensor", "act", psh_free[hp])
                    for lc in range(7):
                        p.run("tensor", lambda ee, l=lc, i2=ic, t2=th, pr=par, q=ph: ee.matmul(
                            q[:], w1sb[pr][:, l * 1024 + i2 * 128:l * 1024 + i2 * 128 + 128],
                            xlat[:, l * 1024 + t2 * 512:l * 1024 + t2 * 512 + 512],
                            start=(l == 0), stop=False))
                    vmm = p.do("tensor", "pe", lambda ee, i2=ic, t2=th, pr=par, q=ph: ee.matmul(
                        q[:], w1sb[pr][:, 7 * 1024 + i2 * 128:7 * 1024 + i2 * 128 + 128],
                        xlat[:, 7 * 1024 + t2 * 512:7 * 1024 + t2 * 512 + 512],
                        start=False, stop=True))
                    p.wait("scalar", "pe", vmm)
                    p.wait("scalar", "dve", vh_last_dve)
                    vre = p.do("scalar", "act", lambda ee, q=ph: ee.activation(
                        t1s[:], q[:], AF.Relu))
                    psh_free[hp] = vre
                    p.wait("vector", "act", vre)
                    vu = p.do("vector", "dve", lambda ee, pr=par, t2=th: ee.tensor_mul(
                        u_s[:], t1s[:], cwb[pr][:, t2 * 512:t2 * 512 + 512]))
                    p.wait("vector", "dve", vu)
                    vh = p.do("vector", "dve", lambda ee, pr=par, i2=ic, t2=th: ee.tensor_mul(
                        hbuf[pr][:, i2 * 1024 + t2 * 512:i2 * 1024 + t2 * 512 + 512],
                        u_s[:], t1s[:]))
                    vh_last_dve = vh
            cwb_free[par] = vh_last_dve
            w_free[par] = 0
            if par == 1:
                p.wait("tensor", "dve", vh_last_dve)
                grp_i = e_ // 2
                for lc in range(8):
                    for th in range(2):
                        yp = (lc * 2 + th) % 2
                        py = psy[yp]
                        p.wait("tensor", "dve", psy_free[yp])
                        nmm = 0
                        for pe2 in range(2):
                            for ic in range(8):
                                nmm += 1
                                f = lambda ee, pr=pe2, i2=ic, l=lc, t2=th, q=py, fr=(nmm == 1), la=(nmm == 16): ee.matmul(
                                    q[:], w2sb[pr][:, i2 * 1024 + l * 128:i2 * 1024 + l * 128 + 128],
                                    hbuf[pr][:, i2 * 1024 + t2 * 512:i2 * 1024 + t2 * 512 + 512],
                                    start=fr, stop=la)
                                if nmm == 16:
                                    vmm = p.do("tensor", "pe", f)
                                else:
                                    p.run("tensor", f)
                        p.wait("vector", "pe", vmm)
                        if grp_i == 0:
                            vev = p.do("vector", "dve", lambda ee, l=lc, t2=th, q=py: ee.tensor_copy(
                                y_sb[:, l * 1024 + t2 * 512:l * 1024 + t2 * 512 + 512], q[:]))
                        else:
                            vev = p.do("vector", "dve", lambda ee, l=lc, t2=th, q=py: ee.tensor_add(
                                y_sb[:, l * 1024 + t2 * 512:l * 1024 + t2 * 512 + 512],
                                y_sb[:, l * 1024 + t2 * 512:l * 1024 + t2 * 512 + 512], q[:]))
                        psy_free[yp] = vev
                w_free[0] = p.val("pe")
                w_free[1] = p.val("pe")
                h_free[0] = p.val("pe")
                h_free[1] = p.val("pe")
        p.wait("vector", "dve", p.val("dve"))
        vyc = p.do("vector", "dve", lambda ee: ee.tensor_copy(ysb_bf[:], y_sb[:]))
        st["p4b_dve"] = p.val("dve")
        st["p4b_pe"] = p.val("pe")
        if "ylat" in dbg:
            p.wait("sync", "dve", p.val("dve"))
            p.dma("sync", "sd", lambda ee: ee.dma_start(out=dbg_bufs["ylat"], in_=y_sb[:]))
        p4ps.close()
        p4sb.close()
        st["stack_xlat"].close()

    def phase5(grp):
        p5sb = ExitStack()
        sb = lambda n, s_, d_: p5sb.enter_context(nc.sbuf_tensor(n, s_, d_))
        p5ps = ExitStack()
        ps = lambda n, s_, d_: p5ps.enter_context(nc.psum_tensor(n, s_, d_))
        ysb_bf = st["ysb_bf"]
        sqs3, st3 = st["sqs3"], st["st3"]

        ffn_bf = sb("p5_ffn", [128, 8 * 2048], BF16)
        s14 = sb("p5_s14", [128, 8], F32)
        s24 = sb("p5_s24", [128, 8], F32)
        red = sb("p5_red", [128, 2], F32)
        stack_sh = ExitStack()
        sh1 = stack_sh.enter_context(nc.sbuf_tensor("p5_sh1", [128, 32 * 1024], BF16))
        s5a = ExitStack()
        xh2 = s5a.enter_context(nc.sbuf_tensor("p5_xh2", [128, 16 * 1024], BF16))
        upt = [s5a.enter_context(nc.sbuf_tensor("p5_up%d" % i, [128, 16 * 128], BF16)) for i in range(2)]
        t15 = s5a.enter_context(nc.sbuf_tensor("p5_t1", [128, 512], F32))
        psu = [ps("p5_pu%d" % i, [128, 512], F32) for i in range(2)]

        p.wait("sync", "dve", st["p4b_dve"])
        p.wait("sync", "pe", st["p4b_pe"])
        p.wait("sync", "sd", st["xln3_park_dma"])
        for kc in range(16):
            v_x3 = p.dma("sync", "sd", lambda e, k=kc: e.dma_start(
                out=xh2[:, k * 1024:(k + 1) * 1024],
                in_=ap["xln3_park"][k * 128:(k + 1) * 128, :]))
        p.wait("gpsimd", "dve", st["p4b_dve"])
        p.wait("gpsimd", "pe", st["p4b_pe"])
        vup = [0, 0]
        up_free = [0, 0]
        psu_free = [0, 0]
        v_sh_dve = 0
        p.wait("tensor", "sd", v_x3)
        p.wait("tensor", "dve", st["p4b_dve"])
        for ic in range(32):
            par = ic % 2
            p.wait("gpsimd", "pe", up_free[par])
            vup[par] = p.dma("gpsimd", "gd", lambda e, i2=ic, pr=par: e.dma_start(
                out=upt[pr][:], in_=ap["upT"][:, i2 * 128:(i2 + 1) * 128].rearrange(
                    "(k p) i -> p k i", p=128)))
            p.wait("tensor", "gd", vup[par])
            for th in range(2):
                up2 = (ic * 2 + th) % 2
                pu = psu[up2]
                p.wait("tensor", "act", psu_free[up2])
                for kc in range(15):
                    p.run("tensor", lambda e, k=kc, i2=ic, t2=th, pr=par, q=pu: e.matmul(
                        q[:], upt[pr][:, k * 128:(k + 1) * 128],
                        xh2[:, k * 1024 + t2 * 512:k * 1024 + t2 * 512 + 512],
                        start=(k == 0), stop=False))
                vmm = p.do("tensor", "pe", lambda e, i2=ic, t2=th, pr=par, q=pu: e.matmul(
                    q[:], upt[pr][:, 15 * 128:16 * 128],
                    xh2[:, 15 * 1024 + t2 * 512:15 * 1024 + t2 * 512 + 512],
                    start=False, stop=True))
                up_free[par] = vmm
                p.wait("scalar", "pe", vmm)
                p.wait("scalar", "dve", v_sh_dve)
                vre = p.do("scalar", "act", lambda e, q=pu: e.activation(t15[:], q[:], AF.Relu))
                psu_free[up2] = vre
                p.wait("vector", "act", vre)
                v_sh_dve = p.do("vector", "dve", lambda e, i2=ic, t2=th: e.tensor_mul(
                    sh1[:, i2 * 1024 + t2 * 512:i2 * 1024 + t2 * 512 + 512], t15[:], t15[:]))
        if "sh1" in dbg:
            p.wait("sync", "dve", p.val("dve"))
            p.dma("sync", "sd", lambda e: e.dma_start(out=dbg_bufs["sh1"], in_=sh1[:]))

        s5a.close()
        if P5STAGE < 2:
            stack_sh.close()
            p5ps.close(); p5sb.close()
            return
        p.wait("gpsimd", "pe", p.val("pe"))
        p.wait("gpsimd", "dve", p.val("dve"))
        p.wait("gpsimd", "act", p.val("act"))
        s5b = ExitStack()
        fc2sb = [s5b.enter_context(nc.sbuf_tensor("p5_f2%d" % i, [128, 8 * 512], BF16)) for i in range(2)]
        dwn = [s5b.enter_context(nc.sbuf_tensor("p5_dw%d" % i, [128, 32 * 512], BF16)) for i in range(2)]
        pso = [ps("p5_po%d" % i, [128, 512], F32) for i in range(2)]

        vdw = [0, 0]
        dw_free = [0, 0]
        pso_free_a = [0, 0]
        pso_free_d = [0, 0]
        v_stats_dve = 0
        v_ffn_act = {}
        for mc in range(4):
            par = mc % 2
            p.wait("gpsimd", "pe", dw_free[par])
            p.dma("gpsimd", "gd", lambda e, m2=mc, pr=par: e.dma_start(
                out=fc2sb[pr][:], in_=ap["fc2T"][:, m2 * 512:(m2 + 1) * 512].rearrange(
                    "(lc p) m -> p lc m", p=128)))
            vdw[par] = p.dma("gpsimd", "gd", lambda e, m2=mc, pr=par: e.dma_start(
                out=dwn[pr][:], in_=ap["downT"][:, m2 * 512:(m2 + 1) * 512].rearrange(
                    "(ic p) m -> p ic m", p=128)))
            p.wait("tensor", "gd", vdw[par])
            if mc == 0:
                p.wait("tensor", "dve", st["p4b_dve"])
            for tc in range(8):
                op2 = (mc * 8 + tc) % 2
                po5 = pso[op2]
                p.wait("tensor", "act", pso_free_a[op2])
                p.wait("tensor", "dve", pso_free_d[op2])
                nmm = 0
                for lc in range(8):
                    nmm += 1
                    p.run("tensor", lambda e, l=lc, t=tc, pr=par, q=po5, fr=(lc == 0): e.matmul(
                        q[:], ysb_bf[:, l * 1024 + t * 128:l * 1024 + t * 128 + 128],
                        fc2sb[pr][:, l * 512:(l + 1) * 512], start=fr, stop=False))
                for ic in range(32):
                    nmm += 1
                    f = lambda e, i2=ic, t=tc, pr=par, q=po5, la=(ic == 31): e.matmul(
                        q[:], sh1[:, i2 * 1024 + t * 128:i2 * 1024 + t * 128 + 128],
                        dwn[pr][:, i2 * 512:(i2 + 1) * 512], start=False, stop=la)
                    if ic == 31:
                        vmm = p.do("tensor", "pe", f)
                    else:
                        p.run("tensor", f)
                dw_free[par] = vmm
                p.wait("scalar", "pe", vmm)
                p.wait("scalar", "dve", v_stats_dve)
                vev = p.do("scalar", "act", lambda e, m2=mc, t=tc, q=po5: e.activation(
                    ffn_bf[:, t * 2048 + m2 * 512:t * 2048 + m2 * 512 + 512], q[:], AF.Identity))
                p.wait("scalar", "act", vev)
                vsq = p.do("scalar", "act", lambda e, q=po5: e.activation(
                    sqs3[:, 0:512], q[:], AF.Square))
                pso_free_a[op2] = vsq
                v_ffn_act[(mc, tc)] = vev
                p.wait("vector", "act", vsq)
                vr1 = p.do("vector", "dve", lambda e, q=po5: e.tensor_reduce(
                    red[:, 0:1], q[:], AX.X, OP.add))
                pso_free_d[op2] = vr1
                p.wait("vector", "act", vsq)
                p.wait("vector", "dve", vr1)
                vr2 = p.do("vector", "dve", lambda e: e.tensor_reduce(
                    red[:, 1:2], sqs3[:, 0:512], AX.X, OP.add))
                p.wait("vector", "dve", vr2)
                if mc == 0:
                    va1 = p.do("vector", "dve", lambda e, t=tc: e.tensor_copy(s14[:, t:t + 1], red[:, 0:1]))
                    p.wait("vector", "dve", va1)
                    va2 = p.do("vector", "dve", lambda e, t=tc: e.tensor_copy(s24[:, t:t + 1], red[:, 1:2]))
                else:
                    va1 = p.do("vector", "dve", lambda e, t=tc: e.tensor_add(
                        s14[:, t:t + 1], s14[:, t:t + 1], red[:, 0:1]))
                    p.wait("vector", "dve", va1)
                    va2 = p.do("vector", "dve", lambda e, t=tc: e.tensor_add(
                        s24[:, t:t + 1], s24[:, t:t + 1], red[:, 1:2]))
                p.wait("vector", "dve", va2)
                v_stats_dve = va2

        s5b.close()
        stack_sh.close()
        if P5STAGE < 3:
            p5ps.close(); p5sb.close()
            return
        x2b = [sb("p5_x2%d" % i, [128, D], F32) for i in range(2)]
        p.wait("sync", "pe", p.val("pe"))
        outb = [sb("p5_ob%d" % i, [128, D], F32) for i in range(2)]
        oscr = sb("p5_oscr", [128, D], F32)
        x2b_free = [0, 0]
        outb_free = [0, 0]
        v_oscr_dve = 0
        for tc in range(8):
            par = tc % 2
            p.wait("sync", "dve", x2b_free[par])
            vx2 = p.dma("sync", "sd", lambda e, t=tc, pr=par: e.dma_start(
                out=x2b[pr][:], in_=ap["x2_park"][t * 128:(t + 1) * 128, :]))
            if tc == 0:
                p.wait("vector", "dve", v_stats_dve)
            vmu = p.do("vector", "dve", lambda e, t=tc: e.tensor_scalar_mul(st3[:, 2:3], s14[:, t:t + 1], 1.0 / D))
            p.wait("vector", "dve", vmu)
            vw = p.do("vector", "dve", lambda e: e.tensor_mul(st3[:, 3:4], st3[:, 2:3], st3[:, 2:3]))
            p.wait("vector", "dve", vw)
            vv = p.do("vector", "dve", lambda e, t=tc: e.scalar_tensor_tensor(
                st3[:, 4:5], s24[:, t:t + 1], 1.0 / D, st3[:, 3:4], OP.mult, OP.subtract))
            p.wait("scalar", "dve", vv)
            vl = p.do("scalar", "act", lambda e: e.activation(st3[:, 5:6], st3[:, 4:5], AF.Ln, bias=eps_ln_sb[:]))
            p.wait("scalar", "act", vl)
            vr = p.do("scalar", "act", lambda e: e.activation(st3[:, 6:7], st3[:, 5:6], AF.Exp, scale=-0.5))
            p.wait("vector", "act", vr)
            vn = p.do("vector", "dve", lambda e: e.scalar_tensor_tensor(
                st3[:, 7:8], st3[:, 2:3], -1.0, st3[:, 6:7], OP.mult, OP.mult))
            p.wait("scalar", "dve", vn)
            p.wait("scalar", "dve", v_oscr_dve)
            vno = p.do("scalar", "act", lambda e, t=tc: e.activation(
                oscr[:], ffn_bf[:, t * 2048:(t + 1) * 2048], AF.Identity,
                bias=st3[:, 7:8], scale=st3[:, 6:7]))
            p.wait("vector", "act", vno)
            p.wait("vector", "sd", vx2)
            p.wait("vector", "sd", outb_free[par])
            vo = p.do("vector", "dve", lambda e, pr=par: e.tensor_add(outb[pr][:], oscr[:], x2b[pr][:]))
            x2b_free[par] = vo
            v_oscr_dve = vo
            p.wait("sync", "dve", vo)
            vod = p.dma("sync", "sd", lambda e, t=tc, pr=par: e.dma_start(
                out=ap["out"][t * 128:(t + 1) * 128, :], in_=outb[pr][:]))
            outb_free[par] = vod
        if "ffn" in dbg:
            p.wait("sync", "act", p.val("act"))
            p.dma("sync", "sd", lambda e: e.dma_start(out=dbg_bufs["ffn"], in_=ffn_bf[:]))
        p5ps.close()
        p5sb.close()

    phase0()
    grpA = ExitStack()
    st["q_sb"] = grpA.enter_context(nc.sbuf_tensor("qsb", [128, 2 * S], BF16))
    st["k_sb"] = grpA.enter_context(nc.sbuf_tensor("ksb", [128, 2 * S], BF16))
    st["v_sb"] = grpA.enter_context(nc.sbuf_tensor("vsb", [128, 64 * 256], BF16))
    st["contrib_sb"] = grpA.enter_context(nc.sbuf_tensor("contribsb", [128, S], BF16))
    mask_sb[0] = grpA.enter_context(nc.sbuf_tensor("mask_sb", [128, 4 * 1024], BF16))
    phase1(grpA)
    phase2(grpA)
    grpA.close()
    grpB = ExitStack()
    if upto >= 3:
        phase3(grpB)
    if upto >= 4:
        phase4a(grpB)
    if upto >= 5:
        phase4b(grpB)
    if upto >= 6:
        phase5(grpB)
    for k_ in ("stack_xlo", "stack_x3", "stack_xlat"):
        if k_ in st:
            st[k_].close()
    grpB.close()

    p.wait("sync", "sd", p.val("sd"))

    with nc.Block() as block:
        @block.tensor
        def _(t):
            for fn in p.ops["tensor"]:
                fn(t, sems)
        @block.scalar
        def _(e):
            for fn in p.ops["scalar"]:
                fn(e, sems)
        @block.vector
        def _(e):
            for fn in p.ops["vector"]:
                fn(e, sems)
        @block.gpsimd
        def _(e):
            for fn in p.ops["gpsimd"]:
                fn(e, sems)
        @block.sync
        def _(e):
            for fn in p.ops["sync"]:
                fn(e, sems)
    ctx.close()
    return nc, dbg_bufs


def kernel(**inputs):
    in_maps = _prep(inputs)
    nc, _ = build_program()
    res = run_bass_kernel_spmd(nc, in_maps, core_ids=list(range(NCORE)))
    out = np.concatenate([r["out"] for r in res.results], axis=0)
    return out.reshape(1, S, D)
